# version 02
# baseline (speedup 1.0000x reference)
import numpy as np
from contextlib import ExitStack
import ml_dtypes
import concourse.bass as bass
import concourse.mybir as mybir
from concourse.bass_utils import run_bass_kernel_spmd

F32 = mybir.dt.float32
BF16 = mybir.dt.bfloat16
I32 = mybir.dt.int32
AF = mybir.ActivationFunctionType
ALU = mybir.AluOpType
AXX = mybir.AxisListType.X
BFNP = ml_dtypes.bfloat16

D = 2048
DCH = 16
OWN = 1024
PRE = 1024
CTX = 2048
TT = 512
NCORE = 8

DC = 512
DR = 128
H0 = 8
HD0 = 64
N0 = 16384
N0CH = 128
H1 = 16
DK = 128
SEG = 512
SC1 = 1.0 / float(np.sqrt(np.float32(DK)))
HK = 5632
HKCH = 44


def cdiv(a, b):
    return (a + b - 1) // b


class Sem:
    def __init__(self, h):
        self.h = h
        self.n = 0

    def inc(self, instr, k=1):
        instr.then_inc(self.h, k)
        self.n += k
        return self.n


class Eng:

    def __init__(self, raw, name, can_drain=True):
        self.raw = raw
        self.name = name
        self.can_drain = can_drain
        self.dirty = set()
        self.last = None

    def _nm(self, ap):
        try:
            return ap.name
        except Exception:
            return None

    def _pre(self, outs, ins):
        names = set()
        for ap in ins + outs:
            n = self._nm(ap)
            if n is not None:
                names.add(n)
        if self.dirty & names:
            if self.can_drain:
                self.raw.drain()
            self.dirty.clear()

    def _post(self, outs, instr=None):
        for ap in outs:
            n = self._nm(ap)
            if n is not None:
                self.dirty.add(n)
        if instr is not None:
            self.last = instr

    def flush(self):
        if self.dirty and self.can_drain:
            self.raw.drain()
        self.dirty.clear()

    def wait(self, sem, v):
        self.raw.wait_ge(sem.h, v)

    def copy(self, out, in_):
        self._pre([out], [in_])
        r = self.raw.tensor_copy(out, in_)
        self._post([out], r)
        return r

    def memset(self, out, v):
        self._pre([out], [])
        r = self.raw.memset(out, v)
        self._post([out], r)
        return r

    def tt(self, out, a, b, op):
        self._pre([out], [a, b])
        r = self.raw.tensor_tensor(out, a, b, op)
        self._post([out], r)
        return r

    def ts(self, out, a, s1, s2, op1, op2=None):
        ins = [a]
        if isinstance(s1, bass.AP):
            ins.append(s1)
        if isinstance(s2, bass.AP):
            ins.append(s2)
        self._pre([out], ins)
        if op2 is None:
            r = self.raw.tensor_scalar(out, a, s1, None, op1)
        else:
            r = self.raw.tensor_scalar(out, a, s1, s2, op1, op2)
        self._post([out])
        return r

    def stt(self, out, in0, scalar, in1, op0, op1):
        ins = [in0, in1]
        if isinstance(scalar, bass.AP):
            ins.append(scalar)
        self._pre([out], ins)
        r = self.raw.scalar_tensor_tensor(out, in0, scalar, in1, op0, op1)
        self._post([out], r)
        return r

    def select(self, out, m, a, b):
        self._pre([out], [m, a, b])
        r = self.raw.select(out, m, a, b)
        self._post([out], r)
        return r

    def relu(self, out, in_):
        self._pre([out], [in_])
        r = self.raw.tensor_relu(out, in_)
        self._post([out], r)
        return r

    def rmax(self, out, in_, negate=False):
        self._pre([out], [in_])
        r = self.raw.reduce_max(out, in_, axis=AXX, negate=negate)
        self._post([out], r)
        return r

    def rsum(self, out, in_, absval=False):
        self._pre([out], [in_])
        r = self.raw.reduce_sum(out, in_, axis=AXX, apply_absolute_value=absval)
        self._post([out], r)
        return r

    def recip(self, out, in_):
        self._pre([out], [in_])
        r = self.raw.reciprocal(out, in_)
        self._post([out], r)
        return r

    def act(self, out, in_, fn, bias=0.0, scale=1.0, accum_out=None):
        ins = [in_]
        if isinstance(bias, bass.AP):
            ins.append(bias)
        outs = [out] + ([accum_out] if accum_out is not None else [])
        self._pre(outs, ins)
        r = self.raw.activation(out, in_, fn, bias=bias, scale=scale,
                                accum_out=accum_out)
        self._post(outs)
        return r

    def iota_sel(self, out, in_, pattern, cmp, fill, base=0, chmul=1):
        self._pre([out], [in_])
        r = self.raw.affine_select(out=out, in_=in_, pattern=pattern,
                                   compare_op=cmp, fill=fill, base=base,
                                   channel_multiplier=chmul)
        self._post([out], r)
        return r


class B:

    def __init__(self, nc, ctx):
        self.nc = nc
        self.ctx = ctx
        E = ctx.enter_context
        self.ve = Eng(nc.vector, "ve")
        self.ac = Eng(nc.scalar, "ac")
        self.gp = Eng(nc.gpsimd, "gp")
        self.te = Eng(nc.tensor, "te", can_drain=False)
        self.sy = Eng(nc.sync, "sy", can_drain=False)
        self.bar = Sem(E(nc.semaphore("bar")))
        self.sdma = Sem(E(nc.semaphore("sdma")))
        self.gdma = Sem(E(nc.semaphore("gdma")))
        self.w0 = Sem(E(nc.semaphore("w0")))
        self.w1 = Sem(E(nc.semaphore("w1")))
        self.spe = Sem(E(nc.semaphore("spe")))
        self.sve = Sem(E(nc.semaphore("sve")))
        self.sac = Sem(E(nc.semaphore("sac")))
        self.sgp = Sem(E(nc.semaphore("sgp")))
        self.sst = Sem(E(nc.semaphore("sst")))
        self.guards = {}

    def guard_wait(self, eng, key):
        g = self.guards.get(key)
        if g is not None:
            eng.wait(g[0], g[1])

    def set_guard(self, key, sem, val):
        self.guards[key] = (sem, val)

    def sb(self, name, shape, dt):
        return self.ctx.enter_context(self.nc.sbuf_tensor(name, shape, dt))

    def load(self, dst, src, eng=None, sem=None):
        e = eng or self.sy
        s = sem or (self.sdma if e is self.sy else self.gdma)
        r = e.raw.dma_start(out=dst, in_=src)
        s.inc(r, 16)
        return s.n

    def store(self, dst, src, eng=None):
        e = eng or self.gp
        r = e.raw.dma_start(out=dst, in_=src)
        self.sst.inc(r, 16)
        return self.sst.n

    def barrier(self, dma=True):
        for e in (self.ve, self.ac, self.gp):
            self.bar.inc(e.raw.drain(), 1)
            e.dirty.clear()
            e.last = None
        if self.te.last is not None:
            self.bar.inc(self.te.last, 1)
            self.te.last = None
        else:
            self.bar.inc(self.te.raw.nop(), 1)
        tgt = self.bar.n
        for e in (self.ve, self.ac, self.gp, self.te):
            e.raw.wait_ge(self.bar.h, tgt)
            if dma:
                e.raw.wait_ge(self.sdma.h, self.sdma.n)
                e.raw.wait_ge(self.gdma.h, self.gdma.n)
            e.dirty.clear()
        self.sy.raw.wait_ge(self.bar.h, tgt)

    def mm(self, out, lhsT, rhs, start=True, stop=True, **kw):
        r = self.te.raw.matmul(out, lhsT, rhs, start=start, stop=stop, **kw)
        self.te.last = r
        return r

    def tr(self, out, in_, ident):
        r = self.te.raw.transpose(out, in_, ident)
        self.te.last = r
        return r

    def pe_done(self):
        if self.te.last is not None:
            v = self.spe.inc(self.te.last, 1)
            self.te.last = None
        else:
            v = self.spe.inc(self.te.raw.nop(), 1)
        return v

    def _eng_done(self, e, sem):
        d = e.raw.drain()
        e.dirty.clear()
        e.last = None
        return sem.inc(d, 1)

    def ve_done(self):
        return self._eng_done(self.ve, self.sve)

    def ac_done(self):
        return self._eng_done(self.ac, self.sac)

    def gp_done(self):
        return self._eng_done(self.gp, self.sgp)



def emit_identity(b, identb, identf):
    b.ve.memset(identf, 1.0)
    v = b.ve_done()
    b.gp.wait(b.sve, v)
    b.gp.iota_sel(identf, identf, [[-1, 128]], ALU.is_equal, 0.0)
    gv = b.gp_done()
    b.ve.wait(b.sgp, gv)
    b.ve.copy(identb, identf)
    b.ve.flush()


def emit_tri_mask(b, tri):
    b.ve.memset(tri, 0.0)
    v = b.ve_done()
    b.gp.wait(b.sve, v)
    b.gp.iota_sel(tri, tri, [[-1, 128]], ALU.is_ge, -1.0e30)
    b.gp_done()


def emit_ln(b, x_sb, xcols, T, xn_sb, xncols, ps_s, ps_q, sqbuf, rows, abuf,
            ones_col, ones_row, pbc):
    nt = cdiv(T, TT)
    for t in range(nt):
        w = min(TT, T - t * TT)
        pe_marks = []
        for c in range(DCH):
            xs = xcols(c, t * TT, w)
            if c >= 2:
                b.ac.wait(b.spe, pe_marks[c - 2])
            b.ac.act(sqbuf[c % 2][:, 0:w], xs, AF.Square)
            av = b.ac_done()
            b.te.wait(b.sac, av)
            b.mm(ps_s[0:1, 0:w], ones_col, xs,
                            start=(c == 0), stop=(c == DCH - 1))
            b.mm(ps_q[0:1, 0:w], ones_col, sqbuf[c % 2][:, 0:w],
                            start=(c == 0), stop=(c == DCH - 1))
            pe_marks.append(b.pe_done())
        v = b.spe.n
        b.ve.wait(b.spe, v)
        mean, var, arow, brow = rows
        b.ve.ts(mean[0:1, 0:w], ps_s[0:1, 0:w], 1.0 / D, None, ALU.mult)
        b.ve.ts(var[0:1, 0:w], ps_q[0:1, 0:w], 1.0 / D, None, ALU.mult)
        b.ve.tt(arow[0:1, 0:w], mean[0:1, 0:w], mean[0:1, 0:w], ALU.mult)
        b.ve.tt(var[0:1, 0:w], var[0:1, 0:w], arow[0:1, 0:w], ALU.subtract)
        b.ve.ts(var[0:1, 0:w], var[0:1, 0:w], 1e-5, None, ALU.add)
        vv = b.ve_done()
        b.ac.wait(b.sve, vv)
        b.ac.act(arow[0:1, 0:w], var[0:1, 0:w], AF.Ln)
        b.ac.act(arow[0:1, 0:w], arow[0:1, 0:w], AF.Exp, scale=-0.5)
        av = b.ac_done()
        b.ve.wait(b.sac, av)
        b.ve.tt(brow[0:1, 0:w], mean[0:1, 0:w], arow[0:1, 0:w], ALU.mult)
        b.ve.ts(brow[0:1, 0:w], brow[0:1, 0:w], -1.0, None, ALU.mult)
        vv = b.ve_done()
        b.te.wait(b.sve, vv)
        b.mm(pbc[:, 0:w], ones_row, arow[0:1, 0:w], start=True, stop=True)
        b.mm(pbc[:, TT:TT + w], ones_row, brow[0:1, 0:w], start=True, stop=True)
        pv = b.pe_done()
        b.ve.wait(b.spe, pv)
        b.ve.copy(abuf[0][:, 0:w], pbc[:, 0:w])
        b.ve.copy(abuf[1][:, 0:w], pbc[:, TT:TT + w])
        b.ve.flush()
        for c in range(DCH):
            xs = xcols(c, t * TT, w)
            xd = xncols(c, t * TT, w)
            b.ve.tt(xd, xs, abuf[0][:, 0:w], ALU.mult)
            b.ve.tt(xd, xd, abuf[1][:, 0:w], ALU.add)
        vv = b.ve_done()
        b.te.wait(b.sve, vv)
    return b.ve_done()


def emit_proj_fm(b, wt_sb, wt_tile, rhs_fn, out_ps, kch, oc_count, Tw,
                 wait_list=None):
    for oc in range(oc_count):
        for kc in range(kch):
            b.mm(out_ps[oc], wt_tile(kc, oc), rhs_fn(kc),
                            start=(kc == 0), stop=(kc == kch - 1))
    return b.pe_done()



def load_wT(b, dst, src, ow, eng=None):
    s3 = src.rearrange("(kc p) o -> p kc o", p=128)
    d3 = dst.rearrange("p (kc o) -> p kc o", o=ow)
    return b.load(d3, s3, eng=(eng or b.gp))



def build_prep():
    nc = bass.Bass("TRN2", target_bir_lowering=False)
    gP = nc.dram_tensor("gP", [2048, 1024], I32, kind="ExternalInput").ap()
    gS = nc.dram_tensor("gS", [2048, 64], F32, kind="ExternalInput").ap()
    uP = nc.dram_tensor("uP", [2048, 1024], I32, kind="ExternalInput").ap()
    uS = nc.dram_tensor("uS", [2048, 64], F32, kind="ExternalInput").ap()
    dP = nc.dram_tensor("dP", [2048, 1024], I32, kind="ExternalInput").ap()
    dS = nc.dram_tensor("dS", [2048, 64], F32, kind="ExternalInput").ap()
    w1 = nc.dram_tensor("w1", [5632, 2048], F32, kind="ExternalInput").ap()
    w2 = nc.dram_tensor("w2", [5632, 2048], F32, kind="ExternalInput").ap()
    w3 = nc.dram_tensor("w3", [2048, 5632], F32, kind="ExternalInput").ap()
    r1 = nc.dram_tensor("r1", [704, 2048], F32, kind="ExternalInput").ap()
    r2 = nc.dram_tensor("r2", [704, 2048], F32, kind="ExternalInput").ap()
    r3 = nc.dram_tensor("r3", [256, 5632], F32, kind="ExternalInput").ap()
    oG = nc.dram_tensor("oG", [2048, 2048], BF16, kind="ExternalOutput").ap()
    oU = nc.dram_tensor("oU", [2048, 2048], BF16, kind="ExternalOutput").ap()
    oD = nc.dram_tensor("oD", [2048, 2048], BF16, kind="ExternalOutput").ap()
    oW1 = nc.dram_tensor("oW1", [704, 2048], BF16, kind="ExternalOutput").ap()
    oW2 = nc.dram_tensor("oW2", [704, 2048], BF16, kind="ExternalOutput").ap()
    oW3 = nc.dram_tensor("oW3", [256, 5632], BF16, kind="ExternalOutput").ap()
    oTHR = nc.dram_tensor("oTHR", [128, 8], F32, kind="ExternalOutput").ap()

    ctx = ExitStack()
    with ctx:
        b = B(nc, ctx)
        pk = b.sb("pk", [128, 1024], I32)
        t0 = b.sb("t0", [128, 1024], I32)
        t1 = b.sb("t1", [128, 1024], I32)
        tl = b.sb("tl", [128, 1024], I32)
        th = b.sb("th", [128, 1024], I32)
        zero = b.sb("zero", [128, 1024], I32)
        scs = b.sb("scs", [128, 64], F32)
        wde = b.sb("wde", [128, 2048], BF16)
        evb = b.sb("evb", [128, 2048], BF16)
        identb = b.sb("identb", [128, 128], BF16)
        identf = b.sb("identf", [128, 128], F32)
        ps = ctx.enter_context(nc.psum_tensor("ps", [128, 1024], BF16))
        ps2 = ctx.enter_context(nc.psum_tensor("ps2", [128, 1024], BF16))

        b.ve.memset(zero[:, :], 0)
        emit_identity(b, identb[:, :], identf[:, :])
        b.barrier()

        def dequant_matrix(src_pk, src_sc, dst):
            for rc in range(16):
                lv = b.load(pk[:, :], src_pk[rc * 128:(rc + 1) * 128, :])
                lv2 = b.load(scs[:, :], src_sc[rc * 128:(rc + 1) * 128, :])
                b.ve.wait(b.sdma, lv2)
                P = pk[:, :]
                b.ve.ts(t0[:, :], P, 7, None, ALU.bitwise_and)
                b.ve.ts(t1[:, :], t0[:, :], 1, None, ALU.is_equal)
                b.ve.tt(t1[:, :], t0[:, :], t1[:, :], ALU.subtract)
                b.ve.ts(t1[:, :], t1[:, :], 64, 0x3F00, ALU.mult, ALU.add)
                b.ve.select(tl[:, :], t0[:, :], t1[:, :], zero[:, :])
                b.ve.ts(t1[:, :], P, 8, 12, ALU.bitwise_and, ALU.logical_shift_left)
                b.ve.tt(tl[:, :], tl[:, :], t1[:, :], ALU.bitwise_or)
                b.ve.ts(t0[:, :], P, 4, 7, ALU.logical_shift_right, ALU.bitwise_and)
                b.ve.ts(t1[:, :], t0[:, :], 1, None, ALU.is_equal)
                b.ve.tt(t1[:, :], t0[:, :], t1[:, :], ALU.subtract)
                b.ve.ts(t1[:, :], t1[:, :], 64, 0x3F00, ALU.mult, ALU.add)
                b.ve.select(th[:, :], t0[:, :], t1[:, :], zero[:, :])
                b.ve.ts(t1[:, :], P, 128, 8, ALU.bitwise_and, ALU.logical_shift_left)
                b.ve.tt(th[:, :], th[:, :], t1[:, :], ALU.bitwise_or)
                b.ve.ts(th[:, :], th[:, :], 16, None, ALU.logical_shift_left)
                b.ve.tt(tl[:, :], tl[:, :], th[:, :], ALU.bitwise_or)
                rb = tl[:, :].bitcast(BF16)
                scb = scs[:, :].unsqueeze(2).broadcast_to([128, 64, 32])
                b.ve.tt(wde[:, :].rearrange("n (g r) -> n g r", r=32),
                        rb.rearrange("n (g r) -> n g r", r=32), scb, ALU.mult)
                vv = b.ve_done()
                b.te.wait(b.sve, vv)
                for cc in range(16):
                    pst = (ps if cc % 2 == 0 else ps2)[:, 0:128]
                    b.te.wait(b.sve, b.sve.n)
                    b.tr(pst, wde[:, cc * 128:(cc + 1) * 128],
                                       identb[:, :])
                    pv = b.pe_done()
                    b.ve.wait(b.spe, pv)
                    b.ve.copy(evb[:, cc * 128:(cc + 1) * 128], pst)
                    b.ve_done()
                vv = b.ve_done()
                b.gp.wait(b.sve, vv)
                for cc in range(16):
                    b.store(dst[cc * 128:(cc + 1) * 128, rc * 128:(rc + 1) * 128],
                            evb[:, cc * 128:(cc + 1) * 128])
                b.barrier()

        dequant_matrix(gP, gS, oG)
        dequant_matrix(uP, uS, oU)
        dequant_matrix(dP, dS, oD)

        wf = b.sb("wf", [128, 2048], F32)
        wf2 = b.sb("wf2", [128, 5632], F32)
        acc = b.sb("acc", [128, 1], F32)
        red = b.sb("red", [128, 1], F32)
        ones_col = b.sb("ones_col", [128, 1], F32)
        ones_row = b.sb("ones_row", [1, 128], F32)
        tot = b.sb("tot", [1, 3], F32)
        thr = b.sb("thr", [128, 3], F32)
        pst1 = ctx.enter_context(nc.psum_tensor("pst1", [1, 512], F32))
        pbc = ctx.enter_context(nc.psum_tensor("pbc", [128, 512], F32))
        b.ve.memset(ones_col[:, :], 1.0)
        b.ve.memset(ones_row[:, :], 1.0)

        for wi, (src, nrows, ncols, buf) in enumerate(
                [(w1, 5632, 2048, wf), (w2, 5632, 2048, wf), (w3, 2048, 5632, wf2)]):
            b.ve.memset(acc[:, :], 0.0)
            b.ve.flush()
            nt = cdiv(nrows, 128)
            for rt in range(nt):
                r0 = rt * 128
                rh = min(128, nrows - r0)
                lv = b.load(buf[0:rh, 0:ncols], src[r0:r0 + rh, :])
                b.ve.wait(b.sdma, lv)
                b.ve.rsum(red[0:rh, 0:1], buf[0:rh, 0:ncols], absval=True)
                b.ve.tt(acc[0:rh, 0:1], acc[0:rh, 0:1], red[0:rh, 0:1], ALU.add)
                vv = b.ve_done()
                b.sy.wait(b.sve, vv)
            vv = b.ve_done()
            b.te.wait(b.sve, vv)
            b.mm(pst1[0:1, 0:1], ones_col[:, 0:1], acc[:, 0:1],
                            start=True, stop=True)
            pv = b.pe_done()
            b.ve.wait(b.spe, pv)
            b.ve.ts(tot[0:1, wi:wi + 1], pst1[0:1, 0:1], 1.0 / (nrows * ncols),
                    None, ALU.mult)
            b.ve.ts(tot[0:1, wi:wi + 1], tot[0:1, wi:wi + 1], 1e-8, None, ALU.add)
            b.ve.ts(tot[0:1, wi:wi + 1], tot[0:1, wi:wi + 1], 0.5, None, ALU.mult)
            vv = b.ve_done()
            b.te.wait(b.sve, vv)
            b.mm(pbc[:, 0:1], ones_row[0:1, :], tot[0:1, wi:wi + 1],
                            start=True, stop=True)
            pv = b.pe_done()
            b.ve.wait(b.spe, pv)
            b.ve.copy(thr[:, wi:wi + 1], pbc[:, 0:1])
            b.ve.flush()
        b.ve.copy(thr[:, 0:3], thr[:, 0:3]) if False else None
        vvd = b.ve_done()
        b.gp.wait(b.sve, vvd)
        b.store(oTHR[:, 0:3], thr[:, 0:3])
        b.store(oTHR[0:1, 3:6], tot[0:1, 0:3])
        b.gp.raw.wait_ge(b.sst.h, b.sst.n)
        b.barrier()

        sg = b.sb("sg", [128, 5632], F32)
        gt = b.sb("gt", [128, 5632], F32)
        qo = b.sb("qo", [128, 5632], BF16)
        for wi, (src, dst, nrows, ncols) in enumerate(
                [(r1, oW1, 704, 2048), (r2, oW2, 704, 2048), (r3, oW3, 256, 5632)]):
            nt = cdiv(nrows, 128)
            for rt in range(nt):
                r0 = rt * 128
                rh = min(128, nrows - r0)
                lv = b.load(wf2[0:rh, 0:ncols], src[r0:r0 + rh, :])
                b.ac.wait(b.sdma, lv)
                b.ac.act(gt[0:rh, 0:ncols], wf2[0:rh, 0:ncols], AF.Abs)
                b.ac.act(sg[0:rh, 0:ncols], wf2[0:rh, 0:ncols], AF.Sign)
                av = b.ac_done()
                b.ve.wait(b.sac, av)
                b.ve.ts(gt[0:rh, 0:ncols], gt[0:rh, 0:ncols], thr[0:rh, wi:wi + 1],
                        None, ALU.subtract)
                vv = b.ve_done()
                b.ac.wait(b.sve, vv)
                b.ac.act(gt[0:rh, 0:ncols], gt[0:rh, 0:ncols], AF.Sign)
                av = b.ac_done()
                b.ve.wait(b.sac, av)
                b.ve.relu(gt[0:rh, 0:ncols], gt[0:rh, 0:ncols])
                b.ve.tt(qo[0:rh, 0:ncols], sg[0:rh, 0:ncols], gt[0:rh, 0:ncols],
                        ALU.mult)
                vv = b.ve_done()
                b.gp.wait(b.sve, vv)
                sv = b.store(dst[r0:r0 + rh, :], qo[0:rh, 0:ncols])
                b.sy.wait(b.sst, sv)
                b.ac.wait(b.sst, sv)
        b.gp.raw.wait_ge(b.sst.h, b.sst.n)
        b.gp.raw.wait_ge(b.gdma.h, b.gdma.n)
    return nc



_LN_CNT = [0]


def emit_ln_stream(b, x_dram, T, xn_put, xbuf, sqbuf, rows, abuf,
                   ones_col, ones_row, wsem):
    _LN_CNT[0] += 1
    tag = _LN_CNT[0]
    ls = ExitStack()
    ps_s = ls.enter_context(b.nc.psum_tensor(f"lnps{tag}", [1, TT], F32))
    ps_q = ls.enter_context(b.nc.psum_tensor(f"lnpq{tag}", [1, TT], F32))
    pbc = ls.enter_context(b.nc.psum_tensor(f"lnpb{tag}", [128, 2 * TT], F32))
    nt = cdiv(T, TT)
    mean, var, arow, brow = rows
    for t in range(nt):
        w = min(TT, T - t * TT)
        t0 = t * TT
        pe_marks = []
        for c in range(DCH):
            par = c % 2
            sem = wsem[par]
            b.guard_wait(b.sy, f"lnx{par}")
            lv = b.load(xbuf[par][:, 0:w], x_dram[c * 128:(c + 1) * 128, t0:t0 + w],
                        eng=b.sy, sem=sem)
            if c >= 2:
                b.ac.wait(b.spe, pe_marks[c - 2])
            b.ac.wait(sem, lv)
            b.ac.act(sqbuf[par][:, 0:w], xbuf[par][:, 0:w], AF.Square)
            av = b.ac_done()
            b.te.wait(b.sac, av)
            b.mm(ps_s[0:1, 0:w], ones_col, xbuf[par][:, 0:w],
                            start=(c == 0), stop=(c == DCH - 1))
            b.mm(ps_q[0:1, 0:w], ones_col, sqbuf[par][:, 0:w],
                            start=(c == 0), stop=(c == DCH - 1))
            pe_marks.append(b.pe_done())
            b.set_guard(f"lnx{par}", b.spe, pe_marks[-1])
        v = b.spe.n
        b.ve.wait(b.spe, v)
        b.ve.ts(mean[0:1, 0:w], ps_s[0:1, 0:w], 1.0 / D, None, ALU.mult)
        b.ve.ts(var[0:1, 0:w], ps_q[0:1, 0:w], 1.0 / D, None, ALU.mult)
        b.ve.tt(arow[0:1, 0:w], mean[0:1, 0:w], mean[0:1, 0:w], ALU.mult)
        b.ve.tt(var[0:1, 0:w], var[0:1, 0:w], arow[0:1, 0:w], ALU.subtract)
        b.ve.ts(var[0:1, 0:w], var[0:1, 0:w], 1e-5, None, ALU.add)
        vv = b.ve_done()
        b.ac.wait(b.sve, vv)
        b.ac.act(arow[0:1, 0:w], var[0:1, 0:w], AF.Ln)
        b.ac.act(arow[0:1, 0:w], arow[0:1, 0:w], AF.Exp, scale=-0.5)
        av = b.ac_done()
        b.ve.wait(b.sac, av)
        b.ve.tt(brow[0:1, 0:w], mean[0:1, 0:w], arow[0:1, 0:w], ALU.mult)
        b.ve.ts(brow[0:1, 0:w], brow[0:1, 0:w], -1.0, None, ALU.mult)
        vv = b.ve_done()
        b.te.wait(b.sve, vv)
        b.mm(pbc[:, 0:w], ones_row, arow[0:1, 0:w], start=True, stop=True)
        b.mm(pbc[:, TT:TT + w], ones_row, brow[0:1, 0:w],
                        start=True, stop=True)
        pv = b.pe_done()
        b.ve.wait(b.spe, pv)
        b.ve.copy(abuf[0][:, 0:w], pbc[:, 0:w])
        b.ve.copy(abuf[1][:, 0:w], pbc[:, TT:TT + w])
        b.ve.flush()
        ve_marks = {}
        for c in range(DCH):
            par = c % 2
            sem = wsem[par]
            b.guard_wait(b.sy, f"lnx{par}")
            lv = b.load(xbuf[par][:, 0:w], x_dram[c * 128:(c + 1) * 128, t0:t0 + w],
                        eng=b.sy, sem=sem)
            b.ve.wait(sem, lv)
            dst = xn_put(c, t0, w)
            b.ve.tt(dst, xbuf[par][:, 0:w], abuf[0][:, 0:w], ALU.mult)
            b.ve.tt(dst, dst, abuf[1][:, 0:w], ALU.add)
            ve_marks[c] = b.ve_done()
            b.set_guard(f"lnx{par}", b.sve, ve_marks[c])
        b.te.wait(b.sve, b.sve.n)
        b.ac.wait(b.sve, b.sve.n)
    ls.close()
    return b.ve_done()



def build_l0(dbg=False):
    nc = bass.Bass("TRN2", target_bir_lowering=False)
    xo = nc.dram_tensor("xo", [D, OWN], F32, kind="ExternalInput").ap()
    xp = nc.dram_tensor("xp", [D, PRE], F32, kind="ExternalInput").ap()
    wCQ = nc.dram_tensor("wCQ", [D, DC], F32, kind="ExternalInput").ap()
    wCQC = nc.dram_tensor("wCQC", [DC, 384], F32, kind="ExternalInput").ap()
    wCQR = nc.dram_tensor("wCQR", [D, DR], F32, kind="ExternalInput").ap()
    wCKV = nc.dram_tensor("wCKV", [D, DC], F32, kind="ExternalInput").ap()
    wCKR = nc.dram_tensor("wCKR", [D, DR], F32, kind="ExternalInput").ap()
    wCKC = nc.dram_tensor("wCKC", [DC, 384], F32, kind="ExternalInput").ap()
    wCV = nc.dram_tensor("wCV", [DC, DC], F32, kind="ExternalInput").ap()
    wOUT = nc.dram_tensor("wOUT", [DC, D], F32, kind="ExternalInput").ap()
    bCQ = nc.dram_tensor("bCQ", [128, 4], F32, kind="ExternalInput").ap()
    bCQC = nc.dram_tensor("bCQC", [128, 3], F32, kind="ExternalInput").ap()
    bCQR = nc.dram_tensor("bCQR", [128, 1], F32, kind="ExternalInput").ap()
    bCKV = nc.dram_tensor("bCKV", [128, 4], F32, kind="ExternalInput").ap()
    bCKR = nc.dram_tensor("bCKR", [128, 1], F32, kind="ExternalInput").ap()
    bCKC = nc.dram_tensor("bCKC", [128, 3], F32, kind="ExternalInput").ap()
    bCV = nc.dram_tensor("bCV", [128, 4], F32, kind="ExternalInput").ap()
    bOUT = nc.dram_tensor("bOUT", [128, 16], F32, kind="ExternalInput").ap()
    rcq = nc.dram_tensor("rcq", [128, OWN], F32, kind="ExternalInput").ap()
    rsq = nc.dram_tensor("rsq", [128, OWN], F32, kind="ExternalInput").ap()
    rcp = nc.dram_tensor("rcp", [128, PRE], F32, kind="ExternalInput").ap()
    rsp = nc.dram_tensor("rsp", [128, PRE], F32, kind="ExternalInput").ap()
    visd = nc.dram_tensor("visd", [128, 1], F32, kind="ExternalInput").ap()
    wg = nc.dram_tensor("wg", [N0CH, 128, DCH, 128], BF16, kind="ExternalInput").ap()
    wu = nc.dram_tensor("wu", [N0CH, 128, DCH, 128], BF16, kind="ExternalInput").ap()
    wd = nc.dram_tensor("wd", [N0CH, 128, DCH, 128], BF16, kind="ExternalInput").ap()
    xout = nc.dram_tensor("xout", [D, OWN], F32, kind="ExternalOutput").ap()
    gdr = nc.dram_tensor("gdr", [D, OWN], BF16, kind="Internal").ap()
    if dbg:
        dXN = nc.dram_tensor("dXN", [D, OWN], F32, kind="ExternalOutput").ap()
        dG = nc.dram_tensor("dG", [D, OWN], F32, kind="ExternalOutput").ap()
        dQ = nc.dram_tensor("dQ", [DC, OWN], F32, kind="ExternalOutput").ap()
        dK = nc.dram_tensor("dK", [DC, CTX], F32, kind="ExternalOutput").ap()
        dV = nc.dram_tensor("dV", [CTX, DC], F32, kind="ExternalOutput").ap()
        dA = nc.dram_tensor("dA", [DC, OWN], F32, kind="ExternalOutput").ap()

    ctx = ExitStack()
    with ctx:
        b = B(nc, ctx)
        xn = b.sb("xn", [128, DCH * OWN], BF16)
        ones_col = b.sb("ones_col", [128, 1], F32)
        ones_row = b.sb("ones_row", [1, 128], F32)
        identb = b.sb("identb", [128, 128], BF16)
        identf = b.sb("identf", [128, 128], F32)
        tri = b.sb("tri", [128, 128], F32)
        vis = b.sb("vis", [128, 1], F32)
        pm = b.sb("pm", [128, 1], F32)
        rowbuf = [b.sb(f"row{i}", [1, TT], F32) for i in range(4)]
        abuf = [b.sb(f"ab{i}", [128, TT], F32) for i in range(2)]
        xbuf = [b.sb(f"xb{i}", [128, TT], F32) for i in range(2)]
        sqbuf = [b.sb(f"sq{i}", [128, TT], F32) for i in range(2)]

        b.ve.memset(ones_col[:, :], 1.0)
        b.ve.memset(ones_row[:, :], 1.0)
        lv = b.load(vis[:, :], visd)
        emit_identity(b, identb[:, :], identf[:, :])
        emit_tri_mask(b, tri[:, :])
        b.ve.wait(b.sdma, lv)
        b.ve.ts(pm[:, 0:1], vis[:, 0:1], -1.0, 1e30, ALU.add, ALU.mult)
        b.ve.flush()
        b.barrier()

        def xn_put(c, t0, w):
            return xn[:, c * OWN + t0: c * OWN + t0 + w]
        emit_ln_stream(b, xo, OWN, xn_put, xbuf, sqbuf, rowbuf,
                       abuf, ones_col[:, 0:1], ones_row[0:1, :], [b.w0, b.w1])
        b.barrier()
        if dbg:
            dxb = b.sb("dxb", [128, OWN], F32)
            for c in range(DCH):
                b.ve.copy(dxb[:, :], xn[:, c * OWN:(c + 1) * OWN])
                vv = b.ve_done()
                b.gp.wait(b.sve, vv)
                b.store(dXN[c * 128:(c + 1) * 128, :], dxb[:, :])
                b.gp.raw.wait_ge(b.sst.h, b.sst.n)
                b.ve.wait(b.sst, b.sst.n)
            b.barrier()

        with ExitStack() as fs:
            def fsb(name, shape, dt):
                return fs.enter_context(nc.sbuf_tensor(name, shape, dt))
            wgb = [fsb(f"wgb{i}", [128, DCH * 128], BF16) for i in range(2)]
            wub = [fsb(f"wub{i}", [128, DCH * 128], BF16) for i in range(2)]
            wdb = [fsb(f"wdb{i}", [128, DCH * 128], BF16) for i in range(2)]
            inter = fsb("inter", [128, 32 * TT], BF16)
            dacc = fsb("dacc", [128, DCH * TT], F32)
            ggb = [fsb(f"ggb{i}", [128, TT], BF16) for i in range(2)]
            gsb = fsb("gsb", [128, TT], BF16)
            psd = [fs.enter_context(nc.psum_tensor(f"psd{i}", [128, TT], F32))
                   for i in range(8)]

            for tt in range(2):
                t0 = tt * TT
                for grp in range(4):
                    pe_marks = {}
                    ve_marks = {}
                    ac_marks = {}
                    for j in range(32):
                        nch = grp * 32 + j
                        par = j % 2
                        sem = [b.w0, b.w1][par]
                        b.guard_wait(b.sy, f"wgu{par}")
                        l1 = b.load(wgb[par][:, :], wg[nch].rearrange("p k n -> p (k n)"),
                                    eng=b.sy, sem=sem)
                        l2 = b.load(wub[par][:, :], wu[nch].rearrange("p k n -> p (k n)"),
                                    eng=b.sy, sem=sem)
                        b.te.wait(sem, l2)
                        if j >= 2:
                            b.te.wait(b.sac, ac_marks[j - 2])
                            b.te.wait(b.sve, ve_marks[j - 2])
                        psg = psd[par]
                        psu = psd[2 + par]
                        for kc in range(DCH):
                            b.mm(psg[:, :], wgb[par][:, kc * 128:(kc + 1) * 128],
                                            xn[:, kc * OWN + t0:kc * OWN + t0 + TT],
                                            start=(kc == 0), stop=(kc == DCH - 1))
                        for kc in range(DCH):
                            b.mm(psu[:, :], wub[par][:, kc * 128:(kc + 1) * 128],
                                            xn[:, kc * OWN + t0:kc * OWN + t0 + TT],
                                            start=(kc == 0), stop=(kc == DCH - 1))
                        pe_marks[j] = b.pe_done()
                        b.set_guard(f"wgu{par}", b.spe, pe_marks[j])
                        b.ac.wait(b.spe, pe_marks[j])
                        if j >= 2:
                            b.ac.wait(b.sve, ve_marks[j - 2])
                        b.ac.act(ggb[par][:, :], psg[:, :], AF.Gelu_apprx_tanh)
                        ac_marks[j] = b.ac_done()
                        b.ve.wait(b.sac, ac_marks[j])
                        b.ve.tt(inter[:, j * TT:(j + 1) * TT], ggb[par][:, :],
                                psu[:, :], ALU.mult)
                        ve_marks[j] = b.ve_done()
                    iv = b.ve_done()
                    b.te.wait(b.sve, iv)
                    for half in range(2):
                        dt0 = half * 8
                        dn_marks = {}
                        for j in range(32):
                            nch = grp * 32 + j
                            par = j % 2
                            sem = [b.w0, b.w1][par]
                            b.guard_wait(b.sy, f"wd{par}")
                            lv = b.load(wdb[par][:, :], wd[nch].rearrange("p k n -> p (k n)"),
                                        eng=b.sy, sem=sem)
                            b.te.wait(sem, lv)
                            for dq in range(8):
                                dt = dt0 + dq
                                b.mm(psd[dq][:, :],
                                                wdb[par][:, dt * 128:(dt + 1) * 128],
                                                inter[:, j * TT:(j + 1) * TT],
                                                start=(j == 0), stop=(j == 31))
                            dn_marks[j] = b.pe_done()
                            b.set_guard(f"wd{par}", b.spe, dn_marks[j])
                        pv = b.spe.n
                        b.ve.wait(b.spe, pv)
                        for dq in range(8):
                            dt = dt0 + dq
                            dst = dacc[:, dt * TT:(dt + 1) * TT]
                            if grp == 0:
                                b.ve.copy(dst, psd[dq][:, :])
                            else:
                                b.ve.tt(dst, dst, psd[dq][:, :], ALU.add)
                        ev = b.ve_done()
                        b.te.wait(b.sve, ev)
                vv = b.ve_done()
                b.ac.wait(b.sve, vv)
                for dt in range(DCH):
                    b.ac.act(gsb[:, :], dacc[:, dt * TT:(dt + 1) * TT], AF.Silu)
                    av = b.ac_done()
                    b.gp.wait(b.sac, av)
                    sv = b.store(gdr[dt * 128:(dt + 1) * 128, t0:t0 + TT], gsb[:, :])
                    b.ac.wait(b.sst, sv)
            b.barrier()

        if dbg:
            gload = b.sb("gload", [128, OWN], BF16)
            gf = b.sb("gf", [128, OWN], F32)
            for c in range(DCH):
                lv = b.load(gload[:, :], gdr[c * 128:(c + 1) * 128, :])
                b.ve.wait(b.sdma, lv)
                b.ve.copy(gf[:, :], gload[:, :])
                vv = b.ve_done()
                b.gp.wait(b.sve, vv)
                b.store(dG[c * 128:(c + 1) * 128, :], gf[:, :])
                b.gp.raw.wait_ge(b.sst.h, b.sst.n)
                b.ve.wait(b.sst, b.sst.n)
                b.sy.wait(b.sst, b.sst.n)
            b.barrier()

        kb = b.sb("kb", [128, 4 * CTX], BF16)
        vb = b.sb("vb", [128, 16 * DC], BF16)
        with ExitStack() as fs:
            def fsb(name, shape, dt):
                return fs.enter_context(nc.sbuf_tensor(name, shape, dt))
            ckvw = fsb("ckvw", [128, DCH * DC], BF16)
            ckrw = fsb("ckrw", [128, DCH * DR], BF16)
            ckcw = fsb("ckcw", [128, 4 * 384], BF16)
            cvw = fsb("cvw", [128, 4 * DC], BF16)
            biasb = fsb("biasb", [128, 16], F32)
            ropeb = fsb("ropeb", [128, 4 * 1024], BF16)
            ckv = fsb("ckv", [128, 4 * TT], BF16)
            xnp = fsb("xnp", [128, DCH * TT], BF16)
            rtmp = fsb("rtmp", [128, TT], F32)
            rtm2 = fsb("rtm2", [128, TT], F32)
            rtm3 = fsb("rtm3", [128, TT], F32)
            psA = fs.enter_context(nc.psum_tensor("psA", [128, TT], F32))
            psB = fs.enter_context(nc.psum_tensor("psB", [128, TT], F32))
            psC = fs.enter_context(nc.psum_tensor("psC", [128, TT], F32))
            psD = fs.enter_context(nc.psum_tensor("psD", [128, TT], F32))

            lw = load_wT(b, ckvw[:, :], wCKV, 512)
            lw = load_wT(b, ckrw[:, :], wCKR, 128)
            lw = load_wT(b, ckcw[:, :], wCKC, 384)
            lw = load_wT(b, cvw[:, :], wCV, 512)
            b.load(biasb[:, 0:4], bCKV)
            b.load(biasb[:, 4:5], bCKR)
            b.load(biasb[:, 5:8], bCKC)
            lw = b.load(ropeb[:, 0:1024], rcp, eng=b.gp)
            lw = b.load(ropeb[:, 1024:2048], rsp, eng=b.gp)
            lw = b.load(ropeb[:, 2048:3072], rcq, eng=b.gp)
            lw = b.load(ropeb[:, 3072:4096], rsq, eng=b.gp)
            b.barrier()

            for st in range(4):
                is_pre = st < 2
                t0 = (st % 2) * TT
                ct0 = st * TT
                if is_pre:
                    def xn_put2(c, tt0, w, _t0=t0):
                        return xnp[:, c * TT: c * TT + w]
                    emit_ln_stream(b, xp[:, t0:t0 + TT],
                                   TT, xn_put2, xbuf, sqbuf,
                                   rowbuf, abuf, ones_col[:, 0:1], ones_row[0:1, :],
                                   [b.w0, b.w1])
                    b.barrier()
                    def xnc(kc):
                        return xnp[:, kc * TT:(kc + 1) * TT]
                else:
                    def xnc(kc, _t0=t0):
                        return xn[:, kc * OWN + _t0: kc * OWN + _t0 + TT]
                for oc in range(4):
                    for kc in range(DCH):
                        b.mm(psA[:, :], ckvw[:, kc * DC + oc * 128:kc * DC + (oc + 1) * 128],
                                        xnc(kc), start=(kc == 0), stop=(kc == DCH - 1))
                    pv = b.pe_done()
                    b.ac.wait(b.spe, pv)
                    b.ac.act(ckv[:, oc * TT:(oc + 1) * TT], psA[:, :], AF.Gelu,
                             bias=biasb[:, oc:oc + 1])
                    av = b.ac_done()
                    b.te.wait(b.sac, av)
                for oc in range(3):
                    for kc in range(4):
                        b.mm(psB[:, :, ] if False else psB[:, :],
                                        ckcw[:, kc * 384 + oc * 128:kc * 384 + (oc + 1) * 128],
                                        ckv[:, kc * TT:(kc + 1) * TT],
                                        start=(kc == 0), stop=(kc == 3))
                    pv = b.pe_done()
                    b.ac.wait(b.spe, pv)
                    b.ac.act(kb[:, oc * CTX + ct0:oc * CTX + ct0 + TT], psB[:, :],
                             AF.Gelu, bias=biasb[:, 5 + oc:6 + oc])
                    av = b.ac_done()
                    b.te.wait(b.sac, av)
                for kc in range(DCH):
                    b.mm(psC[:, :], ckrw[:, kc * DR:(kc + 1) * DR],
                                    xnc(kc), start=(kc == 0), stop=(kc == DCH - 1))
                pv = b.pe_done()
                b.ac.wait(b.spe, pv)
                b.ac.act(rtmp[:, :], psC[:, :], AF.Identity, bias=biasb[:, 4:5])
                av = b.ac_done()
                b.ve.wait(b.sac, av)
                rc_ap = ropeb[:, (0 if is_pre else 2048) + t0:(0 if is_pre else 2048) + t0 + TT]
                rs_ap = ropeb[:, (1024 if is_pre else 3072) + t0:(1024 if is_pre else 3072) + t0 + TT]
                b.ve.copy(rtm3[0:64, :], rtmp[64:128, :])
                b.ve.copy(rtm3[64:128, :], rtmp[0:64, :])
                b.ve.tt(rtm2[:, :], rtmp[:, :], rc_ap, ALU.mult)
                b.ve.tt(rtm3[:, :], rtm3[:, :], rs_ap, ALU.mult)
                b.ve.tt(rtm2[:, :], rtm2[:, :], rtm3[:, :], ALU.add)
                vv = b.ve_done()
                b.ac.wait(b.sve, vv)
                b.ac.act(kb[:, 3 * CTX + ct0:3 * CTX + ct0 + TT], rtm2[:, :], AF.Gelu)
                av = b.ac_done()
                b.te.wait(b.sac, av)
                for sub in range(4):
                    tc = st * 4 + sub
                    for kc in range(4):
                        b.mm(psD[:, :],
                                        ckv[:, kc * TT + sub * 128:kc * TT + (sub + 1) * 128],
                                        cvw[:, kc * DC:(kc + 1) * DC],
                                        start=(kc == 0), stop=(kc == 3))
                    pv = b.pe_done()
                    b.ve.wait(b.spe, pv)
                    b.ve.copy(vb[:, tc * DC:(tc + 1) * DC], psD[:, :])
                    ev = b.ve_done()
                    b.te.wait(b.sve, ev)
            b.barrier()
            if dbg:
                kf = fsb("kf", [128, CTX], F32)
                for c in range(4):
                    b.ve.copy(kf[:, :], kb[:, c * CTX:(c + 1) * CTX])
                    vv = b.ve_done()
                    b.gp.wait(b.sve, vv)
                    b.store(dK[c * 128:(c + 1) * 128, :], kf[:, :])
                    b.gp.raw.wait_ge(b.sst.h, b.sst.n)
                    b.ve.wait(b.sst, b.sst.n)
                for tc2 in range(16):
                    b.ve.copy(kf[:, 0:DC], vb[:, tc2 * DC:(tc2 + 1) * DC])
                    vv = b.ve_done()
                    b.gp.wait(b.sve, vv)
                    b.store(dV[tc2 * 128:(tc2 + 1) * 128, :], kf[:, 0:DC])
                    b.gp.raw.wait_ge(b.sst.h, b.sst.n)
                    b.ve.wait(b.sst, b.sst.n)
                b.barrier()

        qb = b.sb("qb", [128, 4 * OWN], BF16)
        attf = b.sb("attf", [128, 4 * OWN], BF16)
        cvb = b.sb("cvb", [128, 4], F32)
        with ExitStack() as fs:
            def fsb(name, shape, dt):
                return fs.enter_context(nc.sbuf_tensor(name, shape, dt))
            cqw = fsb("cqw", [128, DCH * DC], BF16)
            cqcw = fsb("cqcw", [128, 4 * 384], BF16)
            cqrw = fsb("cqrw", [128, DCH * DR], BF16)
            biasq = fsb("biasq", [128, 8], F32)
            ropeq = fsb("ropeq", [128, 2 * OWN], BF16)
            cq = fsb("cq", [128, 4 * OWN], BF16)
            rtmp = fsb("rtmpq", [128, TT], F32)
            rtm2 = fsb("rtm2q", [128, TT], F32)
            rtm3 = fsb("rtm3q", [128, TT], F32)
            psA = fs.enter_context(nc.psum_tensor("psA2", [128, TT], F32))
            psB = fs.enter_context(nc.psum_tensor("psB2", [128, TT], F32))
            lw = load_wT(b, cqw[:, :], wCQ, 512)
            lw = load_wT(b, cqcw[:, :], wCQC, 384)
            lw = load_wT(b, cqrw[:, :], wCQR, 128)
            b.load(biasq[:, 0:4], bCQ)
            b.load(biasq[:, 4:7], bCQC)
            b.load(biasq[:, 7:8], bCQR)
            lw = b.load(ropeq[:, 0:OWN], rcq, eng=b.gp)
            lw = b.load(ropeq[:, OWN:2 * OWN], rsq, eng=b.gp)
            b.load(cvb[:, :], bCV)
            b.barrier()
            for tt in range(2):
                t0 = tt * TT
                def xnc(kc, _t0=t0):
                    return xn[:, kc * OWN + _t0: kc * OWN + _t0 + TT]
                for oc in range(4):
                    for kc in range(DCH):
                        b.mm(psA[:, :], cqw[:, kc * DC + oc * 128:kc * DC + (oc + 1) * 128],
                                        xnc(kc), start=(kc == 0), stop=(kc == DCH - 1))
                    pv = b.pe_done()
                    b.ac.wait(b.spe, pv)
                    b.ac.act(cq[:, oc * OWN + t0:oc * OWN + t0 + TT], psA[:, :],
                             AF.Gelu, bias=biasq[:, oc:oc + 1])
                    av = b.ac_done()
                    b.te.wait(b.sac, av)
                for oc in range(3):
                    for kc in range(4):
                        b.mm(psB[:, :], cqcw[:, kc * 384 + oc * 128:kc * 384 + (oc + 1) * 128],
                                        cq[:, kc * OWN + t0:kc * OWN + t0 + TT],
                                        start=(kc == 0), stop=(kc == 3))
                    pv = b.pe_done()
                    b.ac.wait(b.spe, pv)
                    b.ac.act(rtmp[:, :], psB[:, :], AF.Gelu, bias=biasq[:, 4 + oc:5 + oc])
                    av = b.ac_done()
                    b.ve.wait(b.sac, av)
                    b.ve.ts(qb[:, oc * OWN + t0:oc * OWN + t0 + TT], rtmp[:, :],
                            0.125, None, ALU.mult)
                    vv = b.ve_done()
                    b.te.wait(b.sve, vv)
                for kc in range(DCH):
                    b.mm(psA[:, :], cqrw[:, kc * DR:(kc + 1) * DR],
                                    xnc(kc), start=(kc == 0), stop=(kc == DCH - 1))
                pv = b.pe_done()
                b.ac.wait(b.spe, pv)
                b.ac.act(rtmp[:, :], psA[:, :], AF.Identity, bias=biasq[:, 7:8])
                av = b.ac_done()
                b.ve.wait(b.sac, av)
                rc_ap = ropeq[:, t0:t0 + TT]
                rs_ap = ropeq[:, OWN + t0:OWN + t0 + TT]
                b.ve.copy(rtm3[0:64, :], rtmp[64:128, :])
                b.ve.copy(rtm3[64:128, :], rtmp[0:64, :])
                b.ve.tt(rtm2[:, :], rtmp[:, :], rc_ap, ALU.mult)
                b.ve.tt(rtm3[:, :], rtm3[:, :], rs_ap, ALU.mult)
                b.ve.tt(rtm2[:, :], rtm2[:, :], rtm3[:, :], ALU.add)
                vv = b.ve_done()
                b.ac.wait(b.sve, vv)
                b.ac.act(rtmp[:, :], rtm2[:, :], AF.Gelu)
                av = b.ac_done()
                b.ve.wait(b.sac, av)
                b.ve.ts(qb[:, 3 * OWN + t0:3 * OWN + t0 + TT], rtmp[:, :],
                        0.125, None, ALU.mult)
                vv = b.ve_done()
                b.te.wait(b.sve, vv)
            b.barrier()
        if dbg:
            qf = b.sb("qf", [128, OWN], F32)
            for c in range(4):
                b.ve.ts(qf[:, :], qb[:, c * OWN:(c + 1) * OWN], 8.0, None, ALU.mult)
                vv = b.ve_done()
                b.gp.wait(b.sve, vv)
                b.store(dQ[c * 128:(c + 1) * 128, :], qf[:, :])
                b.gp.raw.wait_ge(b.sst.h, b.sst.n)
                b.ve.wait(b.sst, b.sst.n)
            b.barrier()

        with ExitStack() as fs:
            def fsb(name, shape, dt):
                return fs.enter_context(nc.sbuf_tensor(name, shape, dt))
            A_sb = fsb("A_sb", [128, CTX], BF16)
            AT_sb = fsb("AT_sb", [128, 8 * 128], BF16)
            mx = fsb("mx", [128, 8], F32)
            sums = fsb("sums", [128, 8], F32)
            smalls = fsb("smalls", [128, 8], F32)
            psP = [fs.enter_context(nc.psum_tensor(f"psP{i}", [128, TT], F32))
                   for i in range(2)]
            psO = [fs.enter_context(nc.psum_tensor(f"psO{i}", [128, TT], F32))
                   for i in range(2)]
            psT = fs.enter_context(nc.psum_tensor("psT", [128, 1024], BF16))
            psV = fs.enter_context(nc.psum_tensor("psV", [64, 128], F32))

            for h in range(H0):
                hc = h // 2
                po = 64 * (h % 2)
                for i in range(8):
                    q_ap = qb[po:po + 64, hc * OWN + i * 128:hc * OWN + (i + 1) * 128]
                    nown = (i + 1) * 128
                    nob = cdiv(nown, TT)
                    for j in range(2):
                        b.mm(psP[j][:, :], q_ap,
                                        kb[po:po + 64, hc * CTX + j * TT:hc * CTX + (j + 1) * TT],
                                        start=True, stop=True)
                    for j in range(nob):
                        wdt = min(TT, nown - j * TT)
                        b.mm(psO[j][:, 0:wdt], q_ap,
                                        kb[po:po + 64, hc * CTX + PRE + j * TT:hc * CTX + PRE + j * TT + wdt],
                                        start=True, stop=True)
                    pv = b.pe_done()
                    b.ve.wait(b.spe, pv)
                    b.ve.ts(psP[0][:, :], psP[0][:, :], pm[:, 0:1], None, ALU.add)
                    b.ve.ts(psP[1][:, :], psP[1][:, :], pm[:, 0:1], None, ALU.add)
                    dj, dcol = divmod(i * 128, TT)
                    b.ve.tt(psO[dj][:, dcol:dcol + 128], psO[dj][:, dcol:dcol + 128],
                            tri[:, :], ALU.add)
                    b.ve.flush()
                    b.ve.rmax(mx[:, 0:1], psP[0][:, :])
                    b.ve.rmax(mx[:, 1:2], psP[1][:, :])
                    for j in range(nob):
                        wdt = min(TT, nown - j * TT)
                        b.ve.rmax(mx[:, 2 + j:3 + j], psO[j][:, 0:wdt])
                    b.ve.rmax(smalls[:, 2:3], mx[:, 0:2 + nob], negate=True)
                    vv = b.ve_done()
                    b.ac.wait(b.sve, vv)
                    b.ac.act(A_sb[:, 0:TT], psP[0][:, :], AF.Exp,
                             bias=smalls[:, 2:3], accum_out=sums[:, 0:1])
                    b.ac.act(A_sb[:, TT:2 * TT], psP[1][:, :], AF.Exp,
                             bias=smalls[:, 2:3], accum_out=sums[:, 1:2])
                    for j in range(nob):
                        wdt = min(TT, nown - j * TT)
                        b.ac.act(A_sb[:, PRE + j * TT:PRE + j * TT + wdt],
                                 psO[j][:, 0:wdt], AF.Exp,
                                 bias=smalls[:, 2:3], accum_out=sums[:, 2 + j:3 + j])
                    av = b.ac_done()
                    b.ve.wait(b.sac, av)
                    b.ve.tt(smalls[:, 3:4], sums[:, 0:1], sums[:, 1:2], ALU.add)
                    for j in range(nob):
                        b.ve.tt(smalls[:, 3:4], smalls[:, 3:4], sums[:, 2 + j:3 + j],
                                ALU.add)
                    b.ve.recip(smalls[:, 4:5], smalls[:, 3:4])
                    b.ve.ts(A_sb[:, 0:PRE + nown], A_sb[:, 0:PRE + nown],
                            smalls[:, 4:5], None, ALU.mult)
                    vv = b.ve_done()
                    b.te.wait(b.sve, vv)
                    nchunks = 8 + (i + 1)
                    first = True
                    done_ch = 0
                    while done_ch < nchunks:
                        wl = min(8, nchunks - done_ch)
                        for t in range(wl):
                            tc = done_ch + t
                            b.tr(psT[:, t * 128:(t + 1) * 128],
                                               A_sb[:, tc * 128:(tc + 1) * 128],
                                               identb[:, :])
                        pv = b.pe_done()
                        b.ve.wait(b.spe, pv)
                        b.ve.copy(AT_sb[:, 0:wl * 128], psT[:, 0:wl * 128])
                        ev = b.ve_done()
                        b.te.wait(b.sve, ev)
                        for t in range(wl):
                            tc = done_ch + t
                            b.mm(psV[:, :],
                                            vb[:, tc * DC + h * 64:tc * DC + (h + 1) * 64],
                                            AT_sb[:, t * 128:(t + 1) * 128],
                                            start=(done_ch + t == 0),
                                            stop=(done_ch + t == nchunks - 1))
                        done_ch += wl
                    pv = b.pe_done()
                    b.ac.wait(b.spe, pv)
                    b.ac.act(attf[po:po + 64, hc * OWN + i * 128:hc * OWN + (i + 1) * 128],
                             psV[0:64, :], AF.Identity, bias=cvb[po:po + 64, hc:hc + 1])
                    av = b.ac_done()
                    b.te.wait(b.sac, av)
                    b.ve.wait(b.sac, av)
            b.barrier()
        if dbg:
            af = b.sb("af", [128, OWN], F32)
            for c in range(4):
                b.ve.copy(af[:, :], attf[:, c * OWN:(c + 1) * OWN])
                vv = b.ve_done()
                b.gp.wait(b.sve, vv)
                b.store(dA[c * 128:(c + 1) * 128, :], af[:, :])
                b.gp.raw.wait_ge(b.sst.h, b.sst.n)
                b.ve.wait(b.sst, b.sst.n)
            b.barrier()

        with ExitStack() as fs:
            def fsb(name, shape, dt):
                return fs.enter_context(nc.sbuf_tensor(name, shape, dt))
            outw = fsb("outw", [128, 4 * D], BF16)
            outb = fsb("outb", [128, 16], F32)
            gld = [fsb(f"gld{i}", [128, TT], BF16) for i in range(2)]
            xld = [fsb(f"xld{i}", [128, TT], F32) for i in range(2)]
            tbuf = [fsb(f"tbuf{i}", [128, TT], F32) for i in range(2)]
            psS = [fs.enter_context(nc.psum_tensor(f"psS{i}", [128, TT], F32))
                   for i in range(2)]
            lw = load_wT(b, outw[:, :], wOUT, 2048)
            b.load(outb[:, :], bOUT)
            b.barrier()
            st_marks = {}
            st_marks_store = {}
            for it, (dt, tt) in enumerate([(dt, tt) for dt in range(DCH)
                                           for tt in range(2)]):
                par = it % 2
                sem = [b.w0, b.w1][par]
                t0 = tt * TT
                if it >= 2:
                    b.sy.wait(b.sve, st_marks[it - 2])
                l1 = b.load(gld[par][:, :], gdr[dt * 128:(dt + 1) * 128, t0:t0 + TT],
                            eng=b.sy, sem=sem)
                l2 = b.load(xld[par][:, :], xo[dt * 128:(dt + 1) * 128, t0:t0 + TT],
                            eng=b.sy, sem=sem)
                if it >= 2:
                    b.te.wait(b.sve, st_marks[it - 2])
                for hv in range(4):
                    b.mm(psS[par][:, :],
                                    outw[:, hv * D + dt * 128:hv * D + (dt + 1) * 128],
                                    attf[:, hv * OWN + t0:hv * OWN + t0 + TT],
                                    start=(hv == 0), stop=(hv == 3))
                pv = b.pe_done()
                b.ve.wait(b.spe, pv)
                b.ve.wait(sem, sem.n)
                if it >= 2:
                    b.ve.wait(b.sst, st_marks_store[it - 2])
                b.ve.stt(tbuf[par][:, :], psS[par][:, :], outb[:, dt:dt + 1],
                         gld[par][:, :], ALU.add, ALU.mult)
                b.ve.tt(tbuf[par][:, :], tbuf[par][:, :], xld[par][:, :], ALU.add)
                st_marks[it] = b.ve_done()
                b.gp.wait(b.sve, st_marks[it])
                sv = b.store(xout[dt * 128:(dt + 1) * 128, t0:t0 + TT], tbuf[par][:, :])
                st_marks_store[it] = sv
            b.gp.raw.wait_ge(b.sst.h, b.sst.n)
            b.gp.raw.wait_ge(b.gdma.h, b.gdma.n)
            b.barrier()
    return nc



def build_l1(dbg=False):
    nc = bass.Bass("TRN2", target_bir_lowering=False)
    xo = nc.dram_tensor("xo", [D, OWN], F32, kind="ExternalInput").ap()
    xp = nc.dram_tensor("xp", [D, PRE], F32, kind="ExternalInput").ap()
    wQ = nc.dram_tensor("wQ", [D, D], F32, kind="ExternalInput").ap()
    wK = nc.dram_tensor("wK", [D, D], F32, kind="ExternalInput").ap()
    wV = nc.dram_tensor("wV", [D, D], F32, kind="ExternalInput").ap()
    wO = nc.dram_tensor("wO", [D, D], F32, kind="ExternalInput").ap()
    betv = nc.dram_tensor("betv", [128, 16], F32, kind="ExternalInput").ap()
    mfl = nc.dram_tensor("mfl", [128, 1], F32, kind="ExternalInput").ap()
    w1t = nc.dram_tensor("w1t", [HKCH, 128, DCH, 128], BF16, kind="ExternalInput").ap()
    w2t = nc.dram_tensor("w2t", [HKCH, 128, DCH, 128], BF16, kind="ExternalInput").ap()
    w3t = nc.dram_tensor("w3t", [HKCH, 128, DCH, 128], BF16, kind="ExternalInput").ap()
    y = nc.dram_tensor("y", [OWN, D], F32, kind="ExternalOutput").ap()
    gdr = nc.dram_tensor("gdr", [D, OWN], BF16, kind="Internal").ap()
    if dbg:
        dXN = nc.dram_tensor("dXN", [D, OWN], F32, kind="ExternalOutput").ap()
        dG = nc.dram_tensor("dG", [D, OWN], F32, kind="ExternalOutput").ap()
        dQG = nc.dram_tensor("dQG", [512, OWN], F32, kind="ExternalOutput").ap()
        dKG = nc.dram_tensor("dKG", [512, CTX], F32, kind="ExternalOutput").ap()
        dVG = nc.dram_tensor("dVG", [CTX, 512], F32, kind="ExternalOutput").ap()
        dAT = nc.dram_tensor("dAT", [D, OWN], F32, kind="ExternalOutput").ap()

    ctx = ExitStack()
    with ctx:
        b = B(nc, ctx)
        xn = b.sb("xn", [128, DCH * OWN], BF16)
        xnp = b.sb("xnp", [128, DCH * PRE], BF16)
        attall = b.sb("attall", [128, H1 * OWN], BF16)
        ones_col = b.sb("ones_col", [128, 1], F32)
        ones_row = b.sb("ones_row", [1, 128], F32)
        onescol_bf = b.sb("onescol_bf", [128, 1], BF16)
        onesrow_bf = b.sb("onesrow_bf", [1, 128], BF16)
        identb = b.sb("identb", [128, 128], BF16)
        identf = b.sb("identf", [128, 128], F32)
        tri = b.sb("tri", [128, 128], F32)
        mflag = b.sb("mflag", [128, 1], F32)
        betas = b.sb("betas", [128, 16], F32)
        rowbuf = [b.sb(f"row{i}", [1, TT], F32) for i in range(4)]
        abuf = [b.sb(f"ab{i}", [128, TT], F32) for i in range(2)]
        xbuf = [b.sb(f"xb{i}", [128, TT], F32) for i in range(2)]
        sqbuf = [b.sb(f"sq{i}", [128, TT], F32) for i in range(2)]

        b.ve.memset(ones_col[:, :], 1.0)
        b.ve.memset(ones_row[:, :], 1.0)
        b.ve.memset(onescol_bf[:, :], 1.0)
        b.ve.memset(onesrow_bf[:, :], 1.0)
        lv = b.load(mflag[:, :], mfl)
        lv = b.load(betas[:, :], betv)
        emit_identity(b, identb[:, :], identf[:, :])
        emit_tri_mask(b, tri[:, :])
        b.ve.wait(b.sdma, b.sdma.n)
        vv = b.ve_done()
        b.ac.wait(b.sve, vv)
        b.ac.act(betas[:, :], betas[:, :], AF.Sigmoid)
        b.ac_done()
        b.barrier()

        def xn_put(c, t0, w):
            return xn[:, c * OWN + t0: c * OWN + t0 + w]
        emit_ln_stream(b, xo, OWN, xn_put, xbuf, sqbuf, rowbuf,
                       abuf, ones_col[:, 0:1], ones_row[0:1, :], [b.w0, b.w1])
        b.barrier()
        if dbg:
            dxb = b.sb("dxb", [128, OWN], F32)
            for c in range(DCH):
                b.ve.copy(dxb[:, :], xn[:, c * OWN:(c + 1) * OWN])
                vv = b.ve_done()
                b.gp.wait(b.sve, vv)
                b.store(dXN[c * 128:(c + 1) * 128, :], dxb[:, :])
                b.gp.raw.wait_ge(b.sst.h, b.sst.n)
                b.ve.wait(b.sst, b.sst.n)
            b.barrier()

        with ExitStack() as fs:
            def fsb(name, shape, dt):
                return fs.enter_context(nc.sbuf_tensor(name, shape, dt))
            w1b = [fsb(f"w1b{i}", [128, DCH * 128], BF16) for i in range(2)]
            w2b = [fsb(f"w2b{i}", [128, DCH * 128], BF16) for i in range(2)]
            w3b = [fsb(f"w3b{i}", [128, DCH * 128], BF16) for i in range(2)]
            hid = fsb("hid", [128, HKCH * TT], BF16)
            sgb = [fsb(f"sgb{i}", [128, TT], BF16) for i in range(2)]
            gsb = fsb("gsb", [128, TT], BF16)
            psd = [fs.enter_context(nc.psum_tensor(f"psd{i}", [128, TT], F32))
                   for i in range(8)]
            for tt in range(2):
                t0 = tt * TT
                pe_marks = {}
                ve_marks = {}
                ac_marks = {}
                for j in range(HKCH):
                    par = j % 2
                    sem = [b.w0, b.w1][par]
                    b.guard_wait(b.sy, f"w12{par}")
                    l1 = b.load(w1b[par][:, :], w1t[j].rearrange("p k n -> p (k n)"),
                                eng=b.sy, sem=sem)
                    l2 = b.load(w2b[par][:, :], w2t[j].rearrange("p k n -> p (k n)"),
                                eng=b.sy, sem=sem)
                    b.te.wait(sem, l2)
                    if j >= 2:
                        b.te.wait(b.sac, ac_marks[j - 2])
                        b.te.wait(b.sve, ve_marks[j - 2])
                    ps1 = psd[par]
                    ps2 = psd[2 + par]
                    for kc in range(DCH):
                        b.mm(ps1[:, :], w1b[par][:, kc * 128:(kc + 1) * 128],
                             xn[:, kc * OWN + t0:kc * OWN + t0 + TT],
                             start=(kc == 0), stop=(kc == DCH - 1))
                    for kc in range(DCH):
                        b.mm(ps2[:, :], w2b[par][:, kc * 128:(kc + 1) * 128],
                             xn[:, kc * OWN + t0:kc * OWN + t0 + TT],
                             start=(kc == 0), stop=(kc == DCH - 1))
                    pe_marks[j] = b.pe_done()
                    b.set_guard(f"w12{par}", b.spe, pe_marks[j])
                    b.ac.wait(b.spe, pe_marks[j])
                    if j >= 2:
                        b.ac.wait(b.sve, ve_marks[j - 2])
                    b.ac.act(sgb[par][:, :], ps1[:, :], AF.Sigmoid)
                    ac_marks[j] = b.ac_done()
                    b.ve.wait(b.sac, ac_marks[j])
                    b.ve.tt(hid[:, j * TT:(j + 1) * TT], sgb[par][:, :],
                            ps2[:, :], ALU.mult)
                    ve_marks[j] = b.ve_done()
                iv = b.ve_done()
                b.te.wait(b.sve, iv)
                for half in range(2):
                    dt0 = half * 8
                    dn_marks = {}
                    for j in range(HKCH):
                        par = j % 2
                        sem = [b.w0, b.w1][par]
                        b.guard_wait(b.sy, f"w3{par}")
                        lv = b.load(w3b[par][:, :], w3t[j].rearrange("p k n -> p (k n)"),
                                    eng=b.sy, sem=sem)
                        b.te.wait(sem, lv)
                        for dq in range(8):
                            dt = dt0 + dq
                            b.mm(psd[dq][:, :], w3b[par][:, dt * 128:(dt + 1) * 128],
                                 hid[:, j * TT:(j + 1) * TT],
                                 start=(j == 0), stop=(j == HKCH - 1))
                        dn_marks[j] = b.pe_done()
                        b.set_guard(f"w3{par}", b.spe, dn_marks[j])
                    pv = b.spe.n
                    b.ac.wait(b.spe, pv)
                    for dq in range(8):
                        dt = dt0 + dq
                        b.ac.act(gsb[:, :], psd[dq][:, :], AF.Silu)
                        av = b.ac_done()
                        b.gp.wait(b.sac, av)
                        sv = b.store(gdr[dt * 128:(dt + 1) * 128, t0:t0 + TT],
                                     gsb[:, :])
                        b.ac.wait(b.sst, sv)
                    av = b.ac_done()
                    b.te.wait(b.sac, av)
            b.barrier()
        if dbg:
            gload = b.sb("gload", [128, OWN], BF16)
            gf = b.sb("gf", [128, OWN], F32)
            for c in range(DCH):
                lv = b.load(gload[:, :], gdr[c * 128:(c + 1) * 128, :])
                b.ve.wait(b.sdma, lv)
                b.ve.copy(gf[:, :], gload[:, :])
                vv = b.ve_done()
                b.gp.wait(b.sve, vv)
                b.store(dG[c * 128:(c + 1) * 128, :], gf[:, :])
                b.gp.raw.wait_ge(b.sst.h, b.sst.n)
                b.ve.wait(b.sst, b.sst.n)
                b.sy.wait(b.sst, b.sst.n)
            b.barrier()

        def xnp_put(c, t0, w):
            return xnp[:, c * PRE + t0: c * PRE + t0 + w]
        emit_ln_stream(b, xp, PRE, xnp_put, xbuf, sqbuf, rowbuf,
                       abuf, ones_col[:, 0:1], ones_row[0:1, :], [b.w0, b.w1])
        b.barrier()

        with ExitStack() as fs:
            def fsb(name, shape, dt):
                return fs.enter_context(nc.sbuf_tensor(name, shape, dt))
            wgb = fsb("wgb", [128, DCH * 512], BF16)
            qg = fsb("qg", [128, 4 * OWN], BF16)
            kg = fsb("kg", [128, 4 * CTX], BF16)
            vg = fsb("vg", [128, 16 * 512], BF16)
            skf = fsb("skf", [128, SEG], BF16)
            sqf = fsb("sqf", [128, SEG], BF16)
            qsc = fsb("qsc", [128, SEG], BF16)
            sk32 = fsb("sk32", [128, SEG], F32)
            sk32b = fsb("sk32b", [128, SEG], F32)
            deltab = fsb("deltab", [128, SEG], BF16)
            skm = fsb("skm", [128, SEG], BF16)
            ATb = fsb("ATb", [128, SEG], BF16)
            A_sb = fsb("A_sb", [128, SEG], BF16)
            amb = fsb("amb", [128, SEG], F32)
            rdb = fsb("rdb", [128, SEG], F32)
            rdr = fsb("rdr", [1, SEG], F32)
            memf = fsb("memf", [128, 128], F32)
            membf = fsb("membf", [128, 128], BF16)
            zf = fsb("zf", [128, 1], F32)
            zbf = fsb("zbf", [128, 1], BF16)
            rzb = fsb("rzb", [128, 8], F32)
            nmx = fsb("nmx", [128, 2], F32)
            sums = fsb("sums", [128, 4], F32)
            psS = fs.enter_context(nc.psum_tensor("psS", [128, TT], F32))
            psTb = fs.enter_context(nc.psum_tensor("psTb", [128, 1024], BF16))
            psD = fs.enter_context(nc.psum_tensor("psD", [128, TT], F32))
            psM = fs.enter_context(nc.psum_tensor("psM", [128, TT], F32))
            psB = fs.enter_context(nc.psum_tensor("psB", [128, TT], F32))
            psR = fs.enter_context(nc.psum_tensor("psR", [1, TT], F32))
            psZ = fs.enter_context(nc.psum_tensor("psZ", [128, 128], F32))
            psU = fs.enter_context(nc.psum_tensor("psU", [128, 128], F32))

            for g in range(4):
                lv = load_wT(b, wgb[:, :], wQ[:, g * 512:(g + 1) * 512], 512)
                b.te.wait(b.gdma, lv)
                for oc in range(4):
                    for tt in range(2):
                        t0 = tt * TT
                        for kc in range(DCH):
                            b.mm(psS[:, :], wgb[:, kc * 512 + oc * 128:kc * 512 + (oc + 1) * 128],
                                 xn[:, kc * OWN + t0:kc * OWN + t0 + TT],
                                 start=(kc == 0), stop=(kc == DCH - 1))
                        pv = b.pe_done()
                        b.ve.wait(b.spe, pv)
                        b.ve.copy(qg[:, oc * OWN + t0:oc * OWN + t0 + TT], psS[:, :])
                        ev = b.ve_done()
                        b.te.wait(b.sve, ev)
                b.barrier()
                lv = load_wT(b, wgb[:, :], wK[:, g * 512:(g + 1) * 512], 512)
                b.te.wait(b.gdma, lv)
                for oc in range(4):
                    for st in range(4):
                        ct0 = st * TT
                        src = xnp if st < 2 else xn
                        s0 = (st % 2) * TT if st < 2 else (st - 2) * TT
                        for kc in range(DCH):
                            b.mm(psS[:, :], wgb[:, kc * 512 + oc * 128:kc * 512 + (oc + 1) * 128],
                                 src[:, kc * (PRE if st < 2 else OWN) + s0:
                                     kc * (PRE if st < 2 else OWN) + s0 + TT],
                                 start=(kc == 0), stop=(kc == DCH - 1))
                        pv = b.pe_done()
                        b.ve.wait(b.spe, pv)
                        b.ve.copy(kg[:, oc * CTX + ct0:oc * CTX + ct0 + TT], psS[:, :])
                        ev = b.ve_done()
                        b.te.wait(b.sve, ev)
                b.barrier()
                lv = load_wT(b, wgb[:, :], wV[:, g * 512:(g + 1) * 512], 512)
                b.te.wait(b.gdma, lv)
                for st in range(4):
                    src = xnp if st < 2 else xn
                    for sub in range(4):
                        tc = st * 4 + sub
                        s0 = ((st % 2) * TT if st < 2 else (st - 2) * TT) + sub * 128
                        for kc in range(DCH):
                            b.mm(psS[:, :],
                                 src[:, kc * (PRE if st < 2 else OWN) + s0:
                                     kc * (PRE if st < 2 else OWN) + s0 + 128],
                                 wgb[:, kc * 512:(kc + 1) * 512],
                                 start=(kc == 0), stop=(kc == DCH - 1))
                        pv = b.pe_done()
                        b.ve.wait(b.spe, pv)
                        b.ve.copy(vg[:, tc * 512:(tc + 1) * 512], psS[:, :])
                        ev = b.ve_done()
                        b.te.wait(b.sve, ev)
                b.barrier()
                if dbg and g == 0:
                    df = fsb(f"df{g}", [128, OWN], F32)
                    for oc in range(4):
                        b.ve.copy(df[:, 0:OWN], qg[:, oc * OWN:(oc + 1) * OWN])
                        vv = b.ve_done()
                        b.gp.wait(b.sve, vv)
                        b.store(dQG[oc * 128:(oc + 1) * 128, :], df[:, 0:OWN])
                        b.gp.raw.wait_ge(b.sst.h, b.sst.n)
                        b.ve.wait(b.sst, b.sst.n)
                    for oc in range(4):
                        for hh in range(2):
                            b.ve.copy(df[:, :], kg[:, oc * CTX + hh * OWN:oc * CTX + (hh + 1) * OWN])
                            vv = b.ve_done()
                            b.gp.wait(b.sve, vv)
                            b.store(dKG[oc * 128:(oc + 1) * 128, hh * OWN:(hh + 1) * OWN], df[:, :])
                            b.gp.raw.wait_ge(b.sst.h, b.sst.n)
                            b.ve.wait(b.sst, b.sst.n)
                    for tc in range(16):
                        b.ve.copy(df[:, 0:512], vg[:, tc * 512:(tc + 1) * 512])
                        vv = b.ve_done()
                        b.gp.wait(b.sve, vv)
                        b.store(dVG[tc * 128:(tc + 1) * 128, :], df[:, 0:512])
                        b.gp.raw.wait_ge(b.sst.h, b.sst.n)
                        b.ve.wait(b.sst, b.sst.n)
                    b.barrier()

                for hl in range(4):
                    h = g * 4 + hl
                    oc = hl
                    b.ve.memset(memf[:, :], 0.0)
                    b.ve.memset(membf[:, :], 0.0)
                    b.ve.memset(zf[:, :], 1.0 / DK)
                    b.ve.memset(zbf[:, :], 1.0 / DK)
                    vv = b.ve_done()
                    b.te.wait(b.sve, vv)
                    for seg in range(4):
                        kk = kg[:, oc * CTX + seg * TT: oc * CTX + (seg + 1) * TT]
                        b.ve.ts(sk32[:, :], kk, 0.0, None, ALU.min)
                        vv = b.ve_done()
                        b.ac.wait(b.sve, vv)
                        b.ac.act(sk32[:, :], sk32[:, :], AF.Exp)
                        av = b.ac_done()
                        b.ve.wait(b.sac, av)
                        b.ve.relu(sk32b[:, :], kk)
                        b.ve.tt(skf[:, :], sk32b[:, :], sk32[:, :], ALU.add)
                        vv = b.ve_done()
                        b.te.wait(b.sve, vv)
                        is_own = seg >= 2
                        if is_own:
                            qq = qg[:, oc * OWN + (seg - 2) * TT: oc * OWN + (seg - 1) * TT]
                            b.ve.ts(sk32[:, :], qq, 0.0, None, ALU.min)
                            vv = b.ve_done()
                            b.ac.wait(b.sve, vv)
                            b.ac.act(sk32[:, :], sk32[:, :], AF.Exp)
                            av = b.ac_done()
                            b.ve.wait(b.sac, av)
                            b.ve.relu(sk32b[:, :], qq)
                            b.ve.tt(sqf[:, :], sk32b[:, :], sk32[:, :], ALU.add)
                            b.ve.ts(qsc[:, :], qq, SC1, None, ALU.mult)
                            vv = b.ve_done()
                            b.te.wait(b.sve, vv)
                            for qi in range(4):
                                wdt = (qi + 1) * 128
                                b.mm(psS[:, 0:wdt], qsc[:, qi * 128:(qi + 1) * 128],
                                     kk[:, 0:wdt], start=True, stop=True)
                                pv = b.pe_done()
                                b.ve.wait(b.spe, pv)
                                b.ve.tt(psS[:, qi * 128:wdt], psS[:, qi * 128:wdt],
                                        tri[:, :], ALU.add)
                                b.ve.rmax(nmx[:, 0:1], psS[:, 0:wdt], negate=True)
                                vv = b.ve_done()
                                b.ac.wait(b.sve, vv)
                                b.ac.act(A_sb[:, 0:wdt], psS[:, 0:wdt], AF.Exp,
                                         bias=nmx[:, 0:1], accum_out=sums[:, 0:1])
                                av = b.ac_done()
                                b.ve.wait(b.sac, av)
                                b.ve.recip(sums[:, 1:2], sums[:, 0:1])
                                b.ve.ts(A_sb[:, 0:wdt], A_sb[:, 0:wdt],
                                        sums[:, 1:2], None, ALU.mult)
                                vv = b.ve_done()
                                b.te.wait(b.sve, vv)
                                for ks in range(qi + 1):
                                    b.tr(psTb[:, ks * 128:(ks + 1) * 128],
                                         A_sb[:, ks * 128:(ks + 1) * 128],
                                         identb[:, :])
                                pv = b.pe_done()
                                b.ve.wait(b.spe, pv)
                                b.ve.copy(ATb[:, 0:wdt], psTb[:, 0:wdt])
                                ev = b.ve_done()
                                b.te.wait(b.sve, ev)
                                for ks in range(qi + 1):
                                    tc = (8 + (seg - 2) * 4) + ks
                                    b.mm(psD[:, qi * 128:(qi + 1) * 128],
                                         vg[:, tc * 512 + oc * 128:tc * 512 + (oc + 1) * 128],
                                         ATb[:, ks * 128:(ks + 1) * 128],
                                         start=(ks == 0), stop=(ks == qi))
                                b.pe_done()
                            b.mm(psR[0:1, :], zbf[:, 0:1], sqf[:, :], start=True, stop=True)
                            pv = b.pe_done()
                            b.ve.wait(b.spe, pv)
                            b.ve.recip(rdr[0:1, :], psR[0:1, :])
                            vv = b.ve_done()
                            b.te.wait(b.sve, vv)
                            b.mm(psB[:, :], ones_row[0:1, :], rdr[0:1, :], start=True, stop=True)
                            b.mm(psM[:, :], membf[:, :], sqf[:, :], start=True, stop=True)
                            pv = b.pe_done()
                            b.ve.wait(b.spe, pv)
                            b.ve.copy(rdb[:, :], psB[:, :])
                            b.ve.tt(amb[:, :], psM[:, :], rdb[:, :], ALU.mult)
                            b.ve.tt(amb[:, :], amb[:, :], psD[:, :], ALU.subtract)
                            b.ve.ts(amb[:, :], amb[:, :], betas[:, h:h + 1], None, ALU.mult)
                            b.ve.tt(attall[:, h * OWN + (seg - 2) * TT:
                                           h * OWN + (seg - 1) * TT],
                                    amb[:, :], psD[:, :], ALU.add)
                            ev = b.ve_done()
                            b.te.wait(b.sve, ev)
                        if seg < 3:
                            for sub in range(4):
                                b.mm(psM[:, sub * 128:(sub + 1) * 128],
                                     skf[:, sub * 128:(sub + 1) * 128], membf[:, :],
                                     start=True, stop=True)
                                b.mm(psZ[:, sub:sub + 1],
                                     skf[:, sub * 128:(sub + 1) * 128], zbf[:, 0:1],
                                     start=True, stop=True)
                            pv = b.pe_done()
                            b.ve.wait(b.spe, pv)
                            for sub in range(4):
                                b.ve.recip(rzb[:, sub:sub + 1], psZ[:, sub:sub + 1])
                                b.ve.ts(rzb[:, sub:sub + 1], rzb[:, sub:sub + 1],
                                        -1.0, None, ALU.mult)
                                tc = seg * 4 + sub
                                b.ve.stt(deltab[:, sub * 128:(sub + 1) * 128],
                                         psM[:, sub * 128:(sub + 1) * 128],
                                         rzb[:, sub:sub + 1],
                                         vg[:, tc * 512 + oc * 128:tc * 512 + (oc + 1) * 128],
                                         ALU.mult, ALU.add)
                            vv = b.ve_done()
                            b.te.wait(b.sve, vv)
                            for sub in range(4):
                                b.tr(psTb[:, sub * 128:(sub + 1) * 128],
                                     skf[:, sub * 128:(sub + 1) * 128], identb[:, :])
                            pv = b.pe_done()
                            b.ve.wait(b.spe, pv)
                            mask_ap = mflag[:, 0:1] if seg < 2 else None
                            for sub in range(4):
                                if mask_ap is not None:
                                    b.ve.ts(skm[:, sub * 128:(sub + 1) * 128],
                                            psTb[:, sub * 128:(sub + 1) * 128],
                                            mask_ap, None, ALU.mult)
                                else:
                                    b.ve.copy(skm[:, sub * 128:(sub + 1) * 128],
                                              psTb[:, sub * 128:(sub + 1) * 128])
                            vv = b.ve_done()
                            b.te.wait(b.sve, vv)
                            for sub in range(4):
                                b.mm(psU[:, :], skm[:, sub * 128:(sub + 1) * 128],
                                     deltab[:, sub * 128:(sub + 1) * 128],
                                     start=(sub == 0), stop=(sub == 3))
                            for sub in range(4):
                                b.mm(psZ[:, 8:9], skm[:, sub * 128:(sub + 1) * 128],
                                     onescol_bf[:, 0:1],
                                     start=(sub == 0), stop=(sub == 3))
                            pv = b.pe_done()
                            b.ve.wait(b.spe, pv)
                            b.ve.tt(memf[:, :], memf[:, :], psU[:, :], ALU.add)
                            b.ve.copy(membf[:, :], memf[:, :])
                            b.ve.tt(zf[:, 0:1], zf[:, 0:1], psZ[:, 8:9], ALU.add)
                            b.ve.copy(zbf[:, 0:1], zf[:, 0:1])
                            vv = b.ve_done()
                            b.te.wait(b.sve, vv)
            b.barrier()
        if dbg:
            af = b.sb("af", [128, OWN], F32)
            for c in range(DCH):
                b.ve.copy(af[:, :], attall[:, c * OWN:(c + 1) * OWN])
                vv = b.ve_done()
                b.gp.wait(b.sve, vv)
                b.store(dAT[c * 128:(c + 1) * 128, :], af[:, :])
                b.gp.raw.wait_ge(b.sst.h, b.sst.n)
                b.ve.wait(b.sst, b.sst.n)
            b.barrier()

        with ExitStack() as fs:
            def fsb(name, shape, dt):
                return fs.enter_context(nc.sbuf_tensor(name, shape, dt))
            wob = [fsb(f"wob{i}", [128, DCH * 128], BF16) for i in range(2)]
            gld = [fsb(f"gld{i}", [128, TT], BF16) for i in range(2)]
            xld = [fsb(f"xld{i}", [128, TT], F32) for i in range(2)]
            tbuf = [fsb(f"tbuf{i}", [128, TT], F32) for i in range(2)]
            outb = fsb("outb", [128, 4 * D], F32)
            psS2 = [fs.enter_context(nc.psum_tensor(f"psS2{i}", [128, TT], F32))
                    for i in range(2)]
            psT2 = fs.enter_context(nc.psum_tensor("psT2", [128, TT], F32))
            for tt in range(2):
                t0 = tt * TT
                for dt in range(DCH):
                    it = tt * DCH + dt
                    par = it % 2
                    sem = [b.w0, b.w1][par]
                    b.guard_wait(b.gp, f"wo{par}")
                    lw = b.load(wob[par][:, :],
                                wO[:, dt * 128:(dt + 1) * 128].rearrange(
                                    "(hc p) o -> p hc o", p=128),
                                eng=b.gp, sem=sem)
                    b.guard_wait(b.sy, f"gx{par}")
                    l1 = b.load(gld[par][:, :], gdr[dt * 128:(dt + 1) * 128, t0:t0 + TT],
                                eng=b.sy, sem=sem)
                    l2 = b.load(xld[par][:, :], xo[dt * 128:(dt + 1) * 128, t0:t0 + TT],
                                eng=b.sy, sem=sem)
                    b.te.wait(sem, l2)
                    b.guard_wait(b.te, f"psS2{par}")
                    attv = attall[:, :].rearrange("p (h s u c) -> p h s u c",
                                                  h=16, s=2, u=32, c=16)
                    for hc in range(DCH):
                        b.mm(psS2[par][:, :], wob[par][:, hc * 128:(hc + 1) * 128],
                             attv[:, :, tt, :, hc],
                             start=(hc == 0), stop=(hc == DCH - 1))
                    pv = b.pe_done()
                    b.set_guard(f"wo{par}", b.spe, pv)
                    b.ve.wait(b.spe, pv)
                    b.ve.stt(tbuf[par][:, :], psS2[par][:, :], 1.0, gld[par][:, :],
                             ALU.mult, ALU.mult)
                    b.ve.tt(tbuf[par][:, :], tbuf[par][:, :], xld[par][:, :], ALU.add)
                    vv = b.ve_done()
                    b.set_guard(f"gx{par}", b.sve, vv)
                    b.set_guard(f"psS2{par}", b.sve, vv)
                    b.te.wait(b.sve, vv)
                    b.guard_wait(b.te, "outtr")
                    for s in range(4):
                        b.tr(psT2[:, s * 128:(s + 1) * 128],
                             tbuf[par][:, s * 128:(s + 1) * 128], identf[:, :])
                    pv = b.pe_done()
                    b.ve.wait(b.spe, pv)
                    for s in range(4):
                        b.ve.copy(outb[:, s * D + dt * 128:s * D + (dt + 1) * 128],
                                  psT2[:, s * 128:(s + 1) * 128])
                    ov = b.ve_done()
                    b.set_guard("outtr", b.sve, ov)
                vv = b.ve_done()
                b.gp.wait(b.sve, vv)
                for s in range(4):
                    sv = b.store(y[t0 + s * 128:t0 + (s + 1) * 128, :],
                                 outb[:, s * D:(s + 1) * D])
                b.ve.wait(b.sst, b.sst.n)
            b.gp.raw.wait_ge(b.sst.h, b.sst.n)
            b.gp.raw.wait_ge(b.gdma.h, b.gdma.n)
            b.barrier()
    return nc



_CACHE = {}


def _get(name, fn):
    if name not in _CACHE:
        _CACHE[name] = fn()
    return _CACHE[name]


def _bias_fm(v):
    return np.ascontiguousarray(np.asarray(v, np.float32).reshape(-1, 128).T)


def _tile_w(WT):
    K, N = WT.shape
    return np.ascontiguousarray(
        WT.reshape(K // 128, 128, N // 128, 128).transpose(2, 1, 0, 3))


def kernel(**inp):
    inp = {k: np.asarray(v) for k, v in inp.items()}
    x = inp["L0_attn_x"]

    nc_p = _get("prep", build_prep)
    in_maps = []
    for c in range(NCORE):
        n0 = c * 2048
        in_maps.append({
            "gP": np.ascontiguousarray(inp["L0_ffn_gate_W_packed"][n0:n0 + 2048]),
            "gS": np.ascontiguousarray(inp["L0_ffn_gate_scales"][n0:n0 + 2048]),
            "uP": np.ascontiguousarray(inp["L0_ffn_up_W_packed"][n0:n0 + 2048]),
            "uS": np.ascontiguousarray(inp["L0_ffn_up_scales"][n0:n0 + 2048]),
            "dP": np.ascontiguousarray(inp["L0_ffn_down_W_packed"][:, c * 1024:(c + 1) * 1024]),
            "dS": np.ascontiguousarray(inp["L0_ffn_down_scales"][:, c * 64:(c + 1) * 64]),
            "w1": inp["L1_ffn_w1"], "w2": inp["L1_ffn_w2"], "w3": inp["L1_ffn_w3"],
            "r1": np.ascontiguousarray(inp["L1_ffn_w1"][c * 704:(c + 1) * 704]),
            "r2": np.ascontiguousarray(inp["L1_ffn_w2"][c * 704:(c + 1) * 704]),
            "r3": np.ascontiguousarray(inp["L1_ffn_w3"][c * 256:(c + 1) * 256]),
        })
    res = run_bass_kernel_spmd(nc_p, in_maps, core_ids=list(range(NCORE)))
    WgT = np.concatenate([res.results[c]["oG"] for c in range(NCORE)], axis=1)
    WuT = np.concatenate([res.results[c]["oU"] for c in range(NCORE)], axis=1)
    WdT = np.concatenate([res.results[c]["oD"] for c in range(NCORE)], axis=0)
    w1q = np.concatenate([res.results[c]["oW1"] for c in range(NCORE)], axis=0)
    w2q = np.concatenate([res.results[c]["oW2"] for c in range(NCORE)], axis=0)
    w3q = np.concatenate([res.results[c]["oW3"] for c in range(NCORE)], axis=0)
    wg_t = _tile_w(WgT)
    wu_t = _tile_w(WuT)
    wd_t = np.ascontiguousarray(WdT.reshape(128, 128, 16, 128))
    w1_t = _tile_w(np.ascontiguousarray(w1q.T))
    w2_t = _tile_w(np.ascontiguousarray(w2q.T))
    w3_t = np.ascontiguousarray(np.ascontiguousarray(w3q.T).reshape(44, 128, 16, 128))

    inv_freq = 1.0 / 10000.0 ** (np.arange(0, 128, 2) / 128.0)
    tpos = np.arange(2048)
    fr = np.einsum("i,j->ij", tpos, inv_freq)
    emb = np.concatenate([fr, fr], -1)
    cosT = np.ascontiguousarray(np.cos(emb).T.astype(np.float32))
    sin_ = np.sin(emb)
    sin_[:, :64] *= -1.0
    sinT = np.ascontiguousarray(sin_.T.astype(np.float32))

    nc_0 = _get("l0", lambda: build_l0(dbg=False))
    in_maps = []
    for c in range(NCORE):
        b2, h = c // 2, c % 2
        own = slice(h * 1024, (h + 1) * 1024)
        pre = slice(0, 1024) if h == 1 else slice(1024, 2048)
        ownpos = np.arange(h * 1024, (h + 1) * 1024)
        in_maps.append({
            "xo": np.ascontiguousarray(x[b2, own].T),
            "xp": np.ascontiguousarray(x[b2, pre].T),
            "wCQ": np.ascontiguousarray(inp["L0_attn_CQ_w"].T),
            "wCQC": np.ascontiguousarray(inp["L0_attn_CQC_w"].T),
            "wCQR": np.ascontiguousarray(inp["L0_attn_CQR_w"].T),
            "wCKV": np.ascontiguousarray(inp["L0_attn_CKV_w"].T),
            "wCKR": np.ascontiguousarray(inp["L0_attn_CKR_w"].T),
            "wCKC": np.ascontiguousarray(inp["L0_attn_CKC_w"].T),
            "wCV": np.ascontiguousarray(inp["L0_attn_CV_w"].T),
            "wOUT": np.ascontiguousarray(inp["L0_attn_OUT_w"].T),
            "bCQ": _bias_fm(inp["L0_attn_CQ_b"]),
            "bCQC": _bias_fm(inp["L0_attn_CQC_b"]),
            "bCQR": _bias_fm(inp["L0_attn_CQR_b"]),
            "bCKV": _bias_fm(inp["L0_attn_CKV_b"]),
            "bCKR": _bias_fm(inp["L0_attn_CKR_b"]),
            "bCKC": _bias_fm(inp["L0_attn_CKC_b"]),
            "bCV": _bias_fm(inp["L0_attn_CV_b"]),
            "bOUT": _bias_fm(inp["L0_attn_OUT_b"]),
            "rcq": np.ascontiguousarray(cosT[:, ownpos]),
            "rsq": np.ascontiguousarray(sinT[:, ownpos]),
            "rcp": np.ascontiguousarray(cosT[:, 0:1024]),
            "rsp": np.ascontiguousarray(sinT[:, 0:1024]),
            "visd": np.full((128, 1), 1.0 if h == 1 else 0.0, np.float32),
            "wg": wg_t, "wu": wu_t, "wd": wd_t,
        })
    res = run_bass_kernel_spmd(nc_0, in_maps, core_ids=list(range(NCORE)))
    x1 = [res.results[c]["xout"] for c in range(NCORE)]

    nc_1 = _get("l1", lambda: build_l1(dbg=False))
    betv = np.ascontiguousarray(inp["L1_attn_betas"][0, :, 0, :].T.astype(np.float32))
    in_maps = []
    for c in range(NCORE):
        b2, h = c // 2, c % 2
        xo_c = x1[c]
        xp_c = x1[2 * b2 + (1 - h)]
        in_maps.append({
            "xo": xo_c, "xp": xp_c,
            "wQ": inp["L1_attn_Wq"], "wK": inp["L1_attn_Wk"],
            "wV": inp["L1_attn_Wv"], "wO": inp["L1_attn_Wo"],
            "betv": betv,
            "mfl": np.full((128, 1), 1.0 if h == 1 else 0.0, np.float32),
            "w1t": w1_t, "w2t": w2_t, "w3t": w3_t,
        })
    res = run_bass_kernel_spmd(nc_1, in_maps, core_ids=list(range(NCORE)))
    out = np.empty((4, 2048, 2048), np.float32)
    for c in range(NCORE):
        b2, h = c // 2, c % 2
        out[b2, h * 1024:(h + 1) * 1024] = res.results[c]["y"]
    return out
